# version 02
# baseline (speedup 1.0000x reference)
import math
import contextlib
import numpy as np
import ml_dtypes
import concourse.bass as bass
import concourse.mybir as mybir
from concourse.bass_utils import run_bass_kernel_spmd

F32 = mybir.dt.float32
BF16 = mybir.dt.bfloat16
U8 = mybir.dt.uint8
AF = mybir.ActivationFunctionType
ALU = mybir.AluOpType
AX = mybir.AxisListType

D = 2048
T = 1024
TF = 2048
FC = 16
NE = 8
DI = 1408
NIC = 11
LAM_INIT = 0.8 - 0.6 * math.exp(-0.3 * 2)
KB = 1024

ARENA_BYTES = 196 * KB
OFF_CONSTS = 0
OFF_SCRATCH = 13 * KB
OFF_WBUF = 29 * KB
OFF_BIG = 73 * KB

CON_SPECS = {
    "idb": ([128, 128], BF16),
    "idf": ([128, 128], F32),
    "ones_f": ([128, 1], F32),
    "onesb_col": ([128, 1], BF16),
    "ones_row_f": ([1, 128], F32),
    "negones_row_f": ([1, 128], F32),
    "ones_row_b": ([1, 128], BF16),
    "basis": ([8, 1024], BF16),
    "prot": ([128, 128], BF16),
    "mask": ([128, 8, 256], BF16),
    "bo": ([128, 16], F32),
    "subw": ([128, 1], F32),
    "sweff": ([128, 1], F32),
    "lamv": ([1, 256], F32),
    "neglam": ([128, 1], F32),
    "small": ([128, 16], F32),
    "combfm": ([8, 1024], BF16),
    "lgall": ([128, 8, 8], F32),
    "wexp": ([128, 8, 8], F32),
    "combtm": ([128, 8, 8], BF16),
    "rmx": ([128, 8, 1], F32),
    "rsm": ([128, 8, 1], F32),
    "rm1": ([128, 8, 1], F32),
    "rm2": ([128, 8, 1], F32),
}

SCR_SPECS = {
    "sq": ([128, 2, 512], F32),
    "sgb": ([128, 2, 512], BF16),
    "ev": ([128, 2, 1024], F32),
    "sums": ([128, 16, 2], F32),
}

WB_SPECS = {
    "wg0": ([128, 16, 256], BF16),
    "wg1": ([128, 16, 256], BF16),
    "wu0": ([128, 16, 256], BF16),
    "wu1": ([128, 16, 256], BF16),
    "wd0": ([128, 11, 256], BF16),
    "wd1": ([128, 11, 256], BF16),
    "rw": ([128, 16, 8], F32),
}

BIG_SPECS = {
    "acc": (0, [128, 16, 1024], F32),
    "xn": (64 * KB, [128, 16, 1024], BF16),
    "h": (96 * KB, [128, 11, 1024], BF16),
    "comb": (118 * KB, [128, 2, 1024], BF16),
    "xoth": (96 * KB, [128, 16, 256], F32),
    "xnoth_st": (112 * KB, [128, 16, 256], BF16),
    "Q": (0, [128, 16, 1024], BF16),
    "K": (32 * KB, [128, 8, 2048], BF16),
    "V": (64 * KB, [128, 16, 1024], BF16),
    "PT": (96 * KB, [128, 16, 512], BF16),
    "P": (112 * KB, [128, 2, 2048], BF16),
    "cstage": (120 * KB, [128, 2, 512], BF16),
    "ropeq": (96 * KB, [128, 2, 1024], BF16),
    "kvin": (96 * KB, [128, 16, 512], BF16),
    "ropek": (112 * KB, [128, 2, 2048], BF16),
    "ctx8": (0, [128, 16, 1024], BF16),
    "gst": (96 * KB, [128, 16, 512], BF16),
    "xst": (48 * KB, [128, 16, 512], F32),
    "ost": (80 * KB, [128, 4, 512], F32),
}


class Build:
    def __init__(self, nc, es):
        self.nc = nc
        self.es = es
        self.sems = {}
        for name in ("s", "g", "t", "v", "a"):
            h = es.enter_context(nc.semaphore(name))
            self.sems[name] = [h, 0]
        self._arena_tensors = []
        self.C = {}
        self.T = {}
        self.PS = {}

    def inc(self, instr, name, n=None):
        rec = self.sems[name]
        if n is None:
            n = 16 if name in ("s", "g") else 1
        rec[1] += n
        instr.then_inc(rec[0], n)
        return rec[1]

    def count(self, name):
        return self.sems[name][1]

    def wait(self, eng, name, val):
        if val > 0:
            eng.wait_ge(self.sems[name][0], val)

    def snap(self):
        return {k: v[1] for k, v in self.sems.items()}

    def waits(self, eng, snap):
        for k, v in snap.items():
            if v > 0:
                eng.wait_ge(self.sems[k][0], v)

    def vhop(self, vector, instr):
        self.inc(instr, "v")
        vector.wait_ge(self.sems["v"][0], self.count("v"))

    def barrier(self):
        s = self.snap()
        for en in ("sync", "gpsimd", "tensor", "vector", "scalar"):
            def f(eng, s=s):
                self.waits(eng, s)
            getattr(self.block, en)(f)
        return s

    def _add(self, name, offset, shape, dtype):
        per_part = mybir.dt.size(dtype)
        for sdim in shape[1:]:
            per_part *= sdim
        assert offset % 32 == 0, (name, offset)
        assert offset + per_part <= ARENA_BYTES, (name, offset, per_part)
        hnd = self.nc.virtual_sbuf_tensor(name, shape, dtype)
        self._arena_tensors.append((hnd, offset))
        return hnd

    def setup(self):
        def place(specs, start, limit, store, prefix):
            off = start
            for name, (shape, dtype) in specs.items():
                pp = mybir.dt.size(dtype)
                for sdim in shape[1:]:
                    pp *= sdim
                store[name] = self._add(prefix + name, off, shape, dtype)
                off += (pp + 31) // 32 * 32
            assert off <= limit, (prefix, off)

        place(CON_SPECS, OFF_CONSTS, OFF_SCRATCH, self.C, "c_")
        place(SCR_SPECS, OFF_SCRATCH, OFF_WBUF, self.T, "s_")
        place(WB_SPECS, OFF_WBUF, OFF_BIG, self.T, "w_")
        for name, (boff, shape, dtype) in BIG_SPECS.items():
            self.T[name] = self._add("b_" + name, OFF_BIG + boff, shape, dtype)
        self.arena = self.es.enter_context(
            self.nc.sbuf_tensor("arena", [128, ARENA_BYTES], U8))
        base = self.nc.lookup_mloc(self.arena).addr
        for hnd, aoff in self._arena_tensors:
            self.nc._tensor(hnd.name, hnd.shape, hnd.dtype, type="SB")
            mloc = self.nc.lookup_mloc(hnd)
            mloc.allocated = True
            mloc.addr = base + aoff
        self.PS["S"] = self.es.enter_context(self.nc.psum_tensor("psS", [128, 2048], F32))
        self.PS["pp0"] = self.es.enter_context(self.nc.psum_tensor("pp0", [128, 512], F32))
        self.PS["pp1"] = self.es.enter_context(self.nc.psum_tensor("pp1", [128, 512], F32))
        self.PS["sp"] = self.es.enter_context(self.nc.psum_tensor("sp", [128, 512], F32))
        self.PS["PB"] = self.es.enter_context(self.nc.psum_tensor("PB", [128, 1024], BF16))
        self.block = self.es.enter_context(self.nc.Block())


def r128(ap):
    return ap.rearrange("(c p) t -> p c t", p=128)



def emit_consts(b, io):
    @b.block.sync
    def _(sync):
        for name in ("idb", "idf", "basis", "prot", "bo", "subw", "lamv"):
            b.inc(sync.dma_start(out=b.C[name][:], in_=io["c_" + name]), "s")
        b.inc(sync.dma_start(out=b.C["mask"][:], in_=io["c_mask"]), "s")

    @b.block.vector
    def _(vector):
        vector.memset(b.C["ones_f"][:], 1.0)
        vector.memset(b.C["onesb_col"][:], 1.0)
        vector.memset(b.C["ones_row_f"][:], 1.0)
        vector.memset(b.C["negones_row_f"][:], -1.0)
        vector.memset(b.C["ones_row_b"][:], 1.0)
        vector.memset(b.C["small"][:], 0.0)
        vector.memset(b.C["small"][:, 0:1], 1e-5)
        b.inc(vector.memset(b.C["small"][:, 1:2], -8.0), "v")

    b.barrier()



def ln_chunk(b, src_tile, out_bf_tile, nT):
    nc = b.nc
    S, pp0, pp1 = b.PS["S"], b.PS["pp0"], b.PS["pp1"]
    sq, ev = b.T["sq"], b.T["ev"]
    a0 = b.count("a")
    t0 = b.count("t")

    @b.block.scalar
    def _(scalar):
        for f in range(FC):
            if f >= 2:
                b.wait(scalar, "t", t0 + (f - 1))
            b.inc(scalar.activation(sq[:, f % 2, 0:nT], src_tile(f), AF.Square), "a")

    @b.block.tensor
    def _(tensor):
        for f in range(FC):
            b.wait(tensor, "a", a0 + f + 1)
            nc.tensor.matmul(S[0:1, 0:nT], b.C["ones_f"][:], src_tile(f),
                             start=(f == 0), stop=(f == FC - 1))
            b.inc(nc.tensor.matmul(S[0:1, 512:512 + nT], b.C["ones_f"][:],
                                   sq[:, f % 2, 0:nT],
                                   start=(f == 0), stop=(f == FC - 1)), "t")

    b.barrier()

    @b.block.vector
    def _(vector):
        m = ev[0:1, 0, 0:nT]
        tmp = ev[0:1, 0, 512:512 + nT]
        var = ev[0:1, 1, 0:nT]
        vector.tensor_scalar(m, S[0:1, 0:nT], 1.0 / D, None, ALU.mult)
        vector.tensor_tensor(tmp, m, m, ALU.mult)
        b.inc(vector.scalar_tensor_tensor(
            var, S[0:1, 512:512 + nT], 1.0 / D, tmp, ALU.mult, ALU.subtract), "v")

    b.barrier()

    @b.block.scalar
    def _(scalar):
        var = ev[0:1, 1, 0:nT]
        lnv = ev[0:1, 1, 512:512 + nT]
        scalar.activation(lnv, var, AF.Ln, bias=b.C["small"][0:1, 0:1], scale=1.0)
        b.inc(scalar.activation(var, lnv, AF.Exp, bias=0.0, scale=-0.5), "a")

    b.barrier()

    @b.block.vector
    def _(vector):
        b.inc(vector.tensor_tensor(ev[0:1, 0, 512:512 + nT], ev[0:1, 0, 0:nT],
                                   ev[0:1, 1, 0:nT], ALU.mult), "v")

    b.barrier()

    @b.block.tensor
    def _(tensor):
        nc.tensor.matmul(pp0[:, 0:nT], b.C["ones_row_f"][:], ev[0:1, 1, 0:nT],
                         start=True, stop=True)
        b.inc(nc.tensor.matmul(pp1[:, 0:nT], b.C["negones_row_f"][:],
                               ev[0:1, 0, 512:512 + nT], start=True, stop=True), "t")

    b.barrier()
    v0 = b.count("v")

    @b.block.vector
    def _(vector):
        for f in range(FC):
            vector.tensor_tensor(src_tile(f), src_tile(f), pp0[:, 0:nT], ALU.mult)
            b.inc(vector.tensor_tensor(src_tile(f), src_tile(f), pp1[:, 0:nT], ALU.add), "v")

    @b.block.scalar
    def _(scalar):
        for f in range(FC):
            b.wait(scalar, "v", v0 + f + 1)
            b.inc(scalar.activation(out_bf_tile(f), src_tile(f), AF.Copy), "a")

    b.barrier()


def emit_p1(b, io, L):
    acc, xn = b.T["acc"], b.T["xn"]
    xoth, xnoth = b.T["xoth"], b.T["xnoth_st"]

    @b.block.sync
    def _(sync):
        b.inc(sync.dma_start(out=acc[:], in_=r128(io["x_own"])), "s")

    b.barrier()
    for tc in range(2):
        ln_chunk(b, lambda f, tc=tc: acc[:, f, 512 * tc:512 * (tc + 1)],
                 lambda f, tc=tc: xn[:, f, 512 * tc:512 * (tc + 1)], 512)

    @b.block.sync
    def _(sync):
        b.inc(sync.dma_start(out=r128(io["xn_own_d"]), in_=xn[:]), "s")

    for tc in range(4):
        @b.block.sync
        def _(sync, tc=tc):
            b.inc(sync.dma_start(
                out=xoth[:], in_=r128(io["x_oth"][:, 256 * tc:256 * (tc + 1)])), "s")

        b.barrier()
        ln_chunk(b, lambda f: xoth[:, f, :], lambda f: xnoth[:, f, :], 256)

        @b.block.sync
        def _(sync, tc=tc):
            b.inc(sync.dma_start(
                out=r128(io["xn_oth_d"][:, 256 * tc:256 * (tc + 1)]), in_=xnoth[:]), "s")

        b.barrier()

    if "d_xn" in io:
        @b.block.sync
        def _(sync):
            b.inc(sync.dma_start(out=r128(io["d_xn"]), in_=xn[:]), "s")
        b.barrier()



def emit_p2(b, io, L):
    nc = b.nc
    S, pp0 = b.PS["S"], b.PS["pp0"]
    acc = b.T["acc"]
    sq = b.T["sq"]
    rw = b.T["rw"]
    lgall, wexp, combtm = b.C["lgall"], b.C["wexp"], b.C["combtm"]

    @b.block.gpsimd
    def _(g):
        b.inc(g.dma_start(out=rw[:], in_=r128(io["w_rwT"])), "g")

    b.barrier()

    @b.block.tensor
    def _(tensor):
        for th in range(2):
            for f in range(FC):
                b.inc(nc.tensor.matmul(
                    S[0:8, 512 * th:512 * (th + 1)], rw[:, f, :],
                    acc[:, f, 512 * th:512 * (th + 1)],
                    start=(f == 0), stop=(f == FC - 1)), "t")

    b.barrier()

    @b.block.scalar
    def _(scalar):
        b.inc(scalar.activation(sq[0:8, 0, :], S[0:8, 0:512], AF.Copy), "a")
        b.inc(scalar.activation(sq[0:8, 1, :], S[0:8, 512:1024], AF.Copy), "a")

    b.barrier()

    @b.block.tensor
    def _(tensor):
        for tt in range(8):
            b.inc(nc.tensor.transpose(
                pp0[:, 8 * tt:8 * (tt + 1)],
                sq[0:8, tt // 4, 128 * (tt % 4):128 * (tt % 4) + 128],
                b.C["idf"][0:8, 0:8]), "t")

    b.barrier()

    @b.block.vector
    def _(vector):
        b.inc(vector.tensor_copy(lgall[:], pp0[:, 0:64]), "v")

    b.barrier()
    if "d_lgt" in io:
        @b.block.sync
        def _(sync):
            b.inc(sync.dma_start(out=io["d_lgt"].rearrange("p (a b) -> p a b", a=8), in_=lgall[:]), "s")
        b.barrier()

    @b.block.vector
    def _(vector):
        mx = b.C["rmx"]
        b.vhop(vector, vector.reduce_max(mx[:], lgall[:], axis=AX.X))
        b.inc(vector.tensor_tensor(lgall[:], lgall[:],
                                   mx[:].to_broadcast([128, 8, 8]), ALU.subtract), "v")

    b.barrier()
    if "d_lgs" in io:
        @b.block.sync
        def _(sync):
            b.inc(sync.dma_start(out=io["d_lgs"].rearrange("p (a b) -> p a b", a=8), in_=lgall[:]), "s")
        b.barrier()

    @b.block.scalar
    def _(scalar):
        b.inc(scalar.activation(wexp[:], lgall[:], AF.Exp), "a")

    b.barrier()

    @b.block.vector
    def _(vector):
        ssum = b.C["rsm"][:]
        sinv = b.C["rmx"][:]
        b.vhop(vector, vector.reduce_sum(ssum, wexp[:], axis=AX.X))
        b.vhop(vector, vector.reciprocal(sinv, ssum))
        b.vhop(vector, vector.tensor_tensor(wexp[:], wexp[:], sinv.to_broadcast([128, 8, 8]), ALU.mult))
        m1 = b.C["rm1"][:]
        m2 = b.C["rm2"][:]
        eq = lgall
        b.vhop(vector, vector.reduce_max(m1, wexp[:], axis=AX.X))
        b.vhop(vector, vector.tensor_tensor(eq[:], wexp[:], m1.to_broadcast([128, 8, 8]), ALU.is_ge))
        b.vhop(vector, vector.scalar_tensor_tensor(eq[:], eq[:], -1e30, wexp[:], ALU.mult, ALU.add))
        b.vhop(vector, vector.reduce_max(m2, eq[:], axis=AX.X))
        b.vhop(vector, vector.tensor_tensor(eq[:], wexp[:], m2.to_broadcast([128, 8, 8]), ALU.is_ge))
        b.inc(vector.tensor_tensor(combtm[:], wexp[:], eq[:], ALU.mult), "v")

    b.barrier()

    @b.block.tensor
    def _(tensor):
        for tt in range(8):
            b.inc(nc.tensor.transpose(
                b.PS["PB"][0:8, 128 * tt:128 * (tt + 1)],
                combtm[:, tt, :], b.C["idb"][:]), "t")

    b.barrier()

    @b.block.scalar
    def _(scalar):
        b.inc(scalar.activation(b.C["combfm"][:], b.PS["PB"][0:8, :], AF.Copy), "a")

    b.barrier()

    if L == 1:
        ev = b.T["ev"]

        @b.block.vector
        def _(vector):
            lamv = b.C["lamv"]
            b.vhop(vector, vector.tensor_tensor(ev[0:1, 0, 0:64], lamv[0:1, 0:64],
                                                lamv[0:1, 64:128], ALU.mult))
            vector.reduce_sum(ev[0:1, 0, 64:65], ev[0:1, 0, 0:64], axis=AX.X)
            b.vhop(vector, vector.tensor_tensor(ev[0:1, 0, 128:192], lamv[0:1, 128:192],
                                                lamv[0:1, 192:256], ALU.mult))
            b.inc(vector.reduce_sum(ev[0:1, 0, 192:193], ev[0:1, 0, 128:192],
                                    axis=AX.X), "v")

        b.barrier()

        @b.block.scalar
        def _(scalar):
            scalar.activation(ev[0:1, 0, 256:257], ev[0:1, 0, 64:65], AF.Exp)
            b.inc(scalar.activation(ev[0:1, 0, 257:258], ev[0:1, 0, 192:193], AF.Exp), "a")

        b.barrier()

        @b.block.vector
        def _(vector):
            b.vhop(vector, vector.tensor_tensor(ev[0:1, 0, 258:259], ev[0:1, 0, 256:257],
                                                ev[0:1, 0, 257:258], ALU.subtract))
            b.inc(vector.tensor_scalar(ev[0:1, 0, 259:260], ev[0:1, 0, 258:259],
                                       LAM_INIT, -1.0, ALU.add, ALU.mult), "v")

        b.barrier()

        @b.block.tensor
        def _(tensor):
            b.inc(nc.tensor.matmul(b.PS["sp"][:, 0:1], b.C["ones_row_f"][:],
                                   ev[0:1, 0, 259:260], start=True, stop=True), "t")

        b.barrier()

        @b.block.vector
        def _(vector):
            vector.tensor_copy(b.C["neglam"][:], b.PS["sp"][:, 0:1])
            b.inc(vector.tensor_scalar(b.C["sweff"][:], b.C["subw"][:],
                                       1.0 - LAM_INIT, None, ALU.mult), "v")

        b.barrier()

    if "d_comb" in io:
        @b.block.sync
        def _(sync):
            b.inc(sync.dma_start(out=io["d_comb"], in_=b.C["combfm"][:]), "s")
        b.barrier()
    if "d_lam" in io and L == 1:
        @b.block.vector
        def _(vector):
            vector.tensor_copy(b.T["ev"][:, 1, 0:1], b.C["neglam"][:])
            vector.tensor_copy(b.T["ev"][:, 1, 1:2], b.C["sweff"][:])
            vector.tensor_copy(b.T["ev"][:, 1, 2:3], b.C["subw"][:])
            b.inc(vector.tensor_copy(b.T["ev"][0:1, 1, 3:8], b.T["ev"][0:1, 0, 256:261]), "v")
        b.barrier()

        @b.block.sync
        def _(sync):
            b.inc(sync.dma_start(out=io["d_lam"], in_=b.T["ev"][:, 1, 0:8]), "s")
        b.barrier()
    if "d_lg" in io:
        @b.block.sync
        def _(sync):
            b.inc(sync.dma_start(out=io["d_lg"].rearrange("e (c t) -> e c t", c=2), in_=sq[0:8, :, :]), "s")
        b.barrier()
    if "d_wexp" in io:
        @b.block.sync
        def _(sync):
            b.inc(sync.dma_start(out=io["d_wexp"].rearrange("p (a b) -> p a b", a=8), in_=wexp[:]), "s")
        b.barrier()



def emit_p3(b, io, L):
    nc = b.nc
    S = b.PS["S"]
    pp = [b.PS["pp0"], b.PS["pp1"]]
    PB = b.PS["PB"]
    acc, xn, h, comb = b.T["acc"], b.T["xn"], b.T["h"], b.T["comb"]
    sgb = b.T["sgb"]
    wg = [b.T["wg0"], b.T["wg1"]]
    wu = [b.T["wu0"], b.T["wu1"]]
    wd = [b.T["wd0"], b.T["wd1"]]
    ICH = [(256 * m, min(256, DI - 256 * m)) for m in range(6)]
    hist = []

    for e in range(9):
        E = b.snap()
        routed = e < 8
        ncomb = 2 if routed else 0
        t_comb = E["t"] + ncomb
        t_gu = lambda u, t_comb=t_comb: t_comb + u + 1
        t_dn = lambda d, t_comb=t_comb: t_comb + 22 + d + 1
        a_comb = E["a"] + ncomb
        a_sig = lambda u, a_comb=a_comb: a_comb + u + 1
        v_h = lambda u, E=E: E["v"] + u + 1
        v_dn = lambda d, E=E: E["v"] + 22 + d + 1
        g_wd = lambda n, E=E: E["g"] + 16 * 12 + 16 * (n + 1)

        @b.block.gpsimd
        def _(g, e=e, E=E, t_gu=t_gu, t_dn=t_dn):
            for m, (i0, ilen) in enumerate(ICH):
                if m >= 2:
                    b.wait(g, "t", t_gu(4 * (m - 2) + 3))
                b.inc(g.dma_start(
                    out=wg[m % 2][:, :, 0:ilen],
                    in_=r128(io["w_gT"][e][:, i0:i0 + ilen])), "g")
                b.inc(g.dma_start(
                    out=wu[m % 2][:, :, 0:ilen],
                    in_=r128(io["w_uT"][e][:, i0:i0 + ilen])), "g")
            for n in range(8):
                if n >= 2:
                    b.wait(g, "t", t_dn(4 * (n - 2) + 3))
                b.inc(g.dma_start(
                    out=wd[n % 2][:],
                    in_=io["w_dT"][e][:, 256 * n:256 * (n + 1)].rearrange(
                        "(c p) o -> p c o", p=128)), "g")

        if routed:
            @b.block.tensor
            def _(tensor, e=e, E=E):
                if e >= 1:
                    b.wait(tensor, "a", hist[e - 1]["a_comb"])
                if e >= 2:
                    b.wait(tensor, "v", hist[e - 2]["v_h21"])
                for half in range(2):
                    if half == 1:
                        b.wait(tensor, "a", E["a"] + 1)
                    b.inc(nc.tensor.matmul(
                        b.PS["sp"][:, :], b.C["basis"][:, 128 * e:128 * (e + 1)],
                        b.C["combfm"][:, 512 * half:512 * (half + 1)],
                        start=True, stop=True), "t")

            @b.block.scalar
            def _(scalar, e=e, E=E):
                for half in range(2):
                    b.wait(scalar, "t", E["t"] + half + 1)
                    b.inc(scalar.activation(
                        comb[:, e % 2, 512 * half:512 * (half + 1)],
                        b.PS["sp"][:, :], AF.Copy), "a")

        @b.block.tensor
        def _(tensor, e=e, E=E, routed=routed, a_comb=a_comb, v_h=v_h):
            for u in range(22):
                ic, th = u // 2, u % 2
                m = ic // 2
                b.wait(tensor, "g", E["g"] + 16 * 2 * (m + 1))
                if routed and u == 0:
                    b.wait(tensor, "a", a_comb)
                if u >= 2:
                    b.wait(tensor, "v", v_h(u - 2))
                icol = 128 * (ic % 2)
                gdst = S[:, 512 * (u % 2):512 * (u % 2) + 512]
                udst = S[:, 1024 + 512 * (u % 2):1024 + 512 * (u % 2) + 512]
                for f in range(FC):
                    nc.tensor.matmul(
                        gdst, wg[m % 2][:, f, icol:icol + 128],
                        xn[:, f, 512 * th:512 * (th + 1)],
                        start=(f == 0), stop=(f == FC - 1))
                for f in range(FC):
                    ins = nc.tensor.matmul(
                        udst, wu[m % 2][:, f, icol:icol + 128],
                        xn[:, f, 512 * th:512 * (th + 1)],
                        start=(f == 0), stop=(f == FC - 1))
                b.inc(ins, "t")

        @b.block.scalar
        def _(scalar, e=e, t_gu=t_gu):
            for u in range(22):
                b.wait(scalar, "t", t_gu(u))
                b.inc(scalar.activation(
                    sgb[:, u % 2, :],
                    S[:, 512 * (u % 2):512 * (u % 2) + 512], AF.Sigmoid), "a")

        @b.block.vector
        def _(vector, e=e, routed=routed, a_sig=a_sig):
            for u in range(22):
                ic, th = u // 2, u % 2
                b.wait(vector, "a", a_sig(u))
                sg = sgb[:, u % 2, :]
                if routed:
                    vector.tensor_tensor(
                        sg, sg, comb[:, e % 2, 512 * th:512 * (th + 1)], ALU.mult)
                b.inc(vector.tensor_tensor(
                    h[:, ic, 512 * th:512 * (th + 1)], sg,
                    S[:, 1024 + 512 * (u % 2):1024 + 512 * (u % 2) + 512],
                    ALU.mult), "v")

        @b.block.tensor
        def _(tensor, e=e, E=E, v_h=v_h, v_dn=v_dn, g_wd=g_wd):
            for d in range(32):
                oc, th = d // 2, d % 2
                n = oc // 2
                b.wait(tensor, "g", g_wd(n))
                if d == 0:
                    b.wait(tensor, "v", v_h(21))
                if d >= 2:
                    b.wait(tensor, "v", v_dn(d - 2))
                ocol = 128 * (oc % 2)
                dst = pp[d % 2]
                for i in range(NIC):
                    ins = nc.tensor.matmul(
                        dst[:, :], wd[n % 2][:, i, ocol:ocol + 128],
                        h[:, i, 512 * th:512 * (th + 1)],
                        start=(i == 0), stop=(i == NIC - 1))
                b.inc(ins, "t")

        @b.block.vector
        def _(vector, e=e, t_dn=t_dn):
            for d in range(32):
                oc, th = d // 2, d % 2
                b.wait(vector, "t", t_dn(d))
                dst = acc[:, oc, 512 * th:512 * (th + 1)]
                b.inc(vector.tensor_tensor(dst, dst, pp[d % 2][:, :], ALU.add), "v")

        hist.append({"a_comb": a_comb, "v_h21": v_h(21)})

    b.barrier()
    if "d_acc" in io:
        @b.block.sync
        def _(sync):
            b.inc(sync.dma_start(out=r128(io["d_acc"]), in_=b.T["acc"][:]), "s")
        b.barrier()



def emit_p4(b, io, L):
    acc, gst = b.T["acc"], b.T["gst"]
    for th in range(2):
        a0 = b.count("a")
        s0 = b.count("s")

        @b.block.scalar
        def _(scalar, th=th, s0=s0):
            if th == 1:
                b.wait(scalar, "s", s0)
            for f in range(FC):
                b.inc(scalar.activation(
                    gst[:, f, :], acc[:, f, 512 * th:512 * (th + 1)], AF.Silu), "a")

        @b.block.sync
        def _(sync, th=th, a0=a0):
            b.wait(sync, "a", a0 + FC)
            b.inc(sync.dma_start(
                out=r128(io["g_d"][:, 512 * th:512 * (th + 1)]), in_=gst[:]), "s")

    b.barrier()



def emit_p5(b, io, L):
    nc = b.nc
    pp = [b.PS["pp0"], b.PS["pp1"]]
    sp = b.PS["sp"]
    xn, Q = b.T["xn"], b.T["Q"]
    ev = b.T["ev"]
    wg = [b.T["wg0"], b.T["wg1"]]
    ropeq = b.T["ropeq"]

    if L == 1:
        @b.block.sync
        def _(sync):
            b.inc(sync.dma_start(out=ropeq[:, 0, :], in_=io["c_cosq"]), "s")
            b.inc(sync.dma_start(out=ropeq[:, 1, :], in_=io["c_sinq"]), "s")

    b.barrier()
    E = b.snap()
    per = 2 if L == 1 else 1
    t_proj = lambda w: E["t"] + per * w + 1
    t_rope = lambda w: E["t"] + per * w + 2
    a_ev = lambda w: E["a"] + w + 1
    v_rope = lambda w: E["v"] + w + 1

    @b.block.gpsimd
    def _(g):
        for m in range(8):
            if m >= 2:
                b.wait(g, "t", t_proj(4 * (m - 2) + 3))
            b.inc(g.dma_start(
                out=wg[m % 2][:], in_=r128(io["w_q"][:, 256 * m:256 * (m + 1)])), "g")

    @b.block.tensor
    def _(tensor):
        for w in range(32):
            oc, th = w // 2, w % 2
            m = oc // 2
            b.wait(tensor, "g", E["g"] + 16 * (m + 1))
            if w >= 2:
                b.wait(tensor, "a", a_ev(w - 2))
            ocol = 128 * (oc % 2)
            for f in range(FC):
                ins = nc.tensor.matmul(
                    pp[w % 2][:, :], wg[m % 2][:, f, ocol:ocol + 128],
                    xn[:, f, 512 * th:512 * (th + 1)],
                    start=(f == 0), stop=(f == FC - 1))
            b.inc(ins, "t")
            if L == 1:
                b.wait(tensor, "a", a_ev(w))
                if w >= 1:
                    b.wait(tensor, "v", v_rope(w - 1))
                b.inc(nc.tensor.matmul(
                    sp[:, :], b.C["prot"][:],
                    Q[:, oc, 512 * th:512 * (th + 1)], start=True, stop=True), "t")

    @b.block.scalar
    def _(scalar):
        for w in range(32):
            oc, th = w // 2, w % 2
            b.wait(scalar, "t", t_proj(w))
            b.inc(scalar.activation(
                Q[:, oc, 512 * th:512 * (th + 1)], pp[w % 2][:, :], AF.Copy), "a")

    if L == 1:
        @b.block.vector
        def _(vector):
            b.wait(vector, "s", E["s"])
            for w in range(32):
                oc, th = w // 2, w % 2
                qc = Q[:, oc, 512 * th:512 * (th + 1)]
                cosc = ropeq[:, 0, 512 * th:512 * (th + 1)]
                sinc = ropeq[:, 1, 512 * th:512 * (th + 1)]
                b.wait(vector, "t", t_rope(w))
                t1 = ev[:, 0, 0:512]
                t2 = ev[:, 1, 0:512]
                vector.tensor_tensor(t1, qc, cosc, ALU.mult)
                vector.tensor_tensor(t2, sp[:, :], sinc, ALU.mult)
                b.inc(vector.tensor_tensor(qc, t1, t2, ALU.add), "v")

    b.barrier()
    if "d_q" in io:
        @b.block.sync
        def _(sync):
            b.inc(sync.dma_start(out=r128(io["d_q"]), in_=b.T["Q"][:]), "s")
        b.barrier()



def emit_p6(b, io, L):
    nc = b.nc
    S = b.PS["S"]
    pp = [b.PS["pp0"], b.PS["pp1"]]
    sp = b.PS["sp"]
    K, V, kvin = b.T["K"], b.T["V"], b.T["kvin"]
    ev = b.T["ev"]
    wg = [b.T["wg0"], b.T["wg1"]]
    wu = [b.T["wu0"], b.T["wu1"]]
    ropek = b.T["ropek"]
    KO = 4 if L == 0 else 8
    NKC = KO // 2
    VO = 512 if L == 0 else 1024
    NVC = VO // 256

    if L == 1:
        @b.block.sync
        def _(sync):
            b.inc(sync.dma_start(out=ropek[:, 0, :], in_=io["c_cosk"]), "s")
            b.inc(sync.dma_start(out=ropek[:, 1, :], in_=io["c_sink"]), "s")

    b.barrier()

    for tc in range(4):
        E = b.snap()
        per = 2 if L == 1 else 1
        t_k = lambda w, E=E: E["t"] + per * w + 1
        a_k = lambda w, E=E: E["a"] + w + 1
        v_rk = lambda w, E=E: E["v"] + w + 1
        t_v = lambda gi, E=E: E["t"] + per * KO + gi + 1
        a_v = lambda gi, E=E: E["a"] + KO + gi + 1

        @b.block.sync
        def _(sync, tc=tc):
            for q in range(4):
                gt = 4 * tc + q
                b.inc(sync.dma_start(
                    out=kvin[:, :, 128 * q:128 * (q + 1)], in_=io["kv_src"][gt]), "s")

        @b.block.gpsimd
        def _(g, tc=tc, E=E):
            for m in range(NKC):
                if m >= 2:
                    b.wait(g, "t", t_k(2 * (m - 2) + 1))
                b.inc(g.dma_start(
                    out=wg[m % 2][:], in_=r128(io["w_k"][:, 256 * m:256 * (m + 1)])), "g")
            for vc in range(NVC):
                if vc >= 2:
                    b.wait(g, "t", t_v(4 * (vc - 2) + 3))
                b.inc(g.dma_start(
                    out=wu[vc % 2][:], in_=r128(io["w_v"][:, 256 * vc:256 * (vc + 1)])), "g")

        @b.block.tensor
        def _(tensor, tc=tc, E=E):
            b.wait(tensor, "s", E["s"] + 16 * 4)
            for w in range(KO):
                oc = w
                m = oc // 2
                b.wait(tensor, "g", E["g"] + 16 * (m + 1))
                if w >= 2:
                    b.wait(tensor, "a", a_k(w - 2))
                ocol = 128 * (oc % 2)
                for f in range(FC):
                    ins = nc.tensor.matmul(
                        pp[w % 2][:, :], wg[m % 2][:, f, ocol:ocol + 128],
                        kvin[:, f, :], start=(f == 0), stop=(f == FC - 1))
                b.inc(ins, "t")
                if L == 1:
                    b.wait(tensor, "a", a_k(w))
                    if w >= 1:
                        b.wait(tensor, "v", v_rk(w - 1))
                    b.inc(nc.tensor.matmul(
                        sp[:, :], b.C["prot"][:],
                        K[:, oc, 512 * tc:512 * (tc + 1)], start=True, stop=True), "t")
            for vc in range(NVC):
                b.wait(tensor, "g", E["g"] + 16 * (NKC + vc + 1))
                for q in range(4):
                    gi = 4 * vc + q
                    if q > 0:
                        b.wait(tensor, "a", a_v(gi - 1))
                    for f in range(FC):
                        ins = nc.tensor.matmul(
                            S[0:128, 512 * vc:512 * vc + 256],
                            kvin[:, f, 128 * q:128 * (q + 1)],
                            wu[vc % 2][:, f, :],
                            start=(f == 0), stop=(f == FC - 1))
                    b.inc(ins, "t")

        @b.block.scalar
        def _(scalar, tc=tc, E=E):
            for w in range(KO):
                b.wait(scalar, "t", t_k(w))
                b.inc(scalar.activation(
                    K[:, w, 512 * tc:512 * (tc + 1)], pp[w % 2][:, :], AF.Copy), "a")
            for vc in range(NVC):
                for q in range(4):
                    gi = 4 * vc + q
                    b.wait(scalar, "t", t_v(gi))
                    b.inc(scalar.activation(
                        V[:, 4 * tc + q, 256 * vc:256 * (vc + 1)],
                        S[0:128, 512 * vc:512 * vc + 256], AF.Copy), "a")

        if L == 1:
            @b.block.vector
            def _(vector, tc=tc, E=E):
                for w in range(KO):
                    kc = K[:, w, 512 * tc:512 * (tc + 1)]
                    cosc = ropek[:, 0, 512 * tc:512 * (tc + 1)]
                    sinc = ropek[:, 1, 512 * tc:512 * (tc + 1)]
                    b.wait(vector, "t", t_k(w) + 1)
                    t1 = ev[:, 0, 0:512]
                    t2 = ev[:, 1, 0:512]
                    vector.tensor_tensor(t1, kc, cosc, ALU.mult)
                    vector.tensor_tensor(t2, sp[:, :], sinc, ALU.mult)
                    b.inc(vector.tensor_tensor(kc, t1, t2, ALU.add), "v")

        b.barrier()

    if "d_k" in io:
        @b.block.sync
        def _(sync):
            b.inc(sync.dma_start(
                out=r128(io["d_k"]), in_=b.T["K"][:, 0:(4 if L == 0 else 8), :]), "s")
        b.barrier()
    if "d_v" in io:
        @b.block.sync
        def _(sync):
            b.inc(sync.dma_start(out=r128(io["d_v"]), in_=b.T["V"][:]), "s")
        b.barrier()



def emit_p7(b, io, L):
    nc = b.nc
    S = b.PS["S"]
    pp = [b.PS["pp0"], b.PS["pp1"]]
    sp = b.PS["sp"]
    PB = b.PS["PB"]
    Q, K, V, PT, P = b.T["Q"], b.T["K"], b.T["V"], b.T["PT"], b.T["P"]
    cstage = b.T["cstage"]
    sgb = b.T["sgb"]
    ev = b.T["ev"]
    sums = b.T["sums"]
    NH = 16
    scale = (1.0 / math.sqrt(128.0)) if L == 0 else (1.0 / math.sqrt(64.0))
    nvar = 1 if L == 0 else 2
    pvi = 0
    prev_ctx_done = [None, None]
    prev_cst_dma = [0, 0]
    prev_pb_free = 0

    for H in range(2):
        for h in range(NH):
            exts = [256 * (4 * H + j + 1) for j in range(4)]
            maxc = exts[3] // 128

            if L == 0:
                g_ = h // 4
                lhs_qs = [lambda j, h=h, H=H:
                          Q[:, h, 128 * (4 * H + j):128 * (4 * H + j) + 128]]
                key_ap = lambda c0, cl, g_=g_: K[:, g_, c0:c0 + cl]
                vcol = 128 * g_
            else:
                kch, kbase = h // 2, 64 * (h % 2)
                key_ap = lambda c0, cl, kch=kch, kbase=kbase: \
                    K[kbase:kbase + 64, kch, c0:c0 + cl]
                qch0 = 2 * (h // 2)
                lhs_qs = [
                    lambda j, qch0=qch0, kbase=kbase, H=H:
                        Q[kbase:kbase + 64, qch0, 128 * (4 * H + j):128 * (4 * H + j) + 128],
                    lambda j, qch0=qch0, kbase=kbase, H=H:
                        Q[kbase:kbase + 64, qch0 + 1, 128 * (4 * H + j):128 * (4 * H + j) + 128],
                ]
                vcol = 128 * (h // 2)

            groups_all = []
            for j in range(4):
                ext = exts[j]
                for vi in range(nvar):
                    tE = b.count("t")
                    vE = b.count("v")
                    aE = b.count("a")

                    @b.block.tensor
                    def _(tensor, vi=vi, j=j, ext=ext, aE=aE, H=H):
                        b.wait(tensor, "a", aE)
                        c0 = 0
                        ins = None
                        while c0 < ext:
                            cl = min(512, ext - c0)
                            ins = nc.tensor.matmul(
                                S[0:128, c0:c0 + cl], lhs_qs[vi](j),
                                key_ap(c0, cl), start=True, stop=True)
                            c0 += cl
                        b.inc(ins, "t")

                    @b.block.vector
                    def _(vector, j=j, ext=ext, tE=tE, H=H):
                        b.wait(vector, "t", tE + 1)
                        b.inc(vector.tensor_tensor(
                            S[0:128, ext - 256:ext], S[0:128, ext - 256:ext],
                            b.C["mask"][:, 4 * H + j, :], ALU.add), "v")

                    @b.block.scalar
                    def _(scalar, vi=vi, j=j, ext=ext, vE=vE):
                        b.wait(scalar, "v", vE + 1)
                        b.inc(scalar.activation(
                            P[:, vi, 0:ext], S[0:128, 0:ext], AF.Exp,
                            bias=b.C["small"][:, 1:2], scale=scale,
                            accum_out=sums[:, 4 * vi + j, 0:1]), "a")

                aEx = b.count("a")

                @b.block.vector
                def _(vector, j=j, ext=ext, aEx=aEx):
                    b.wait(vector, "a", aEx)
                    ins = vector.reciprocal(sums[:, j, 1:2], sums[:, j, 0:1])
                    if L == 1:
                        ins = vector.reciprocal(sums[:, 4 + j, 1:2], sums[:, 4 + j, 0:1])
                    b.inc(ins, "v")
                    vector.wait_ge(b.sems["v"][0], b.count("v"))
                    inv0 = sums[:, j, 1:2]
                    if L == 0:
                        b.inc(vector.tensor_scalar(
                            P[:, 0, 0:ext], P[:, 0, 0:ext], inv0, None, ALU.mult), "v")
                    else:
                        inv1 = sums[:, 4 + j, 1:2]
                        b.vhop(vector, vector.tensor_tensor(
                            inv1, inv1, b.C["neglam"][:], ALU.mult))
                        vector.tensor_scalar(
                            P[:, 0, 0:ext], P[:, 0, 0:ext], inv0, None, ALU.mult)
                        b.inc(vector.scalar_tensor_tensor(
                            P[:, 0, 0:ext], P[:, 1, 0:ext], inv1,
                            P[:, 0, 0:ext], ALU.mult, ALU.add), "v")

                vN = b.count("v")
                nch = ext // 128
                tB = b.count("t")
                aB = b.count("a")
                gidx = 0
                slot_groups = [(g0, min(4, nch - g0)) for g0 in range(0, nch, 4)]
                groups_all.extend(slot_groups)

                @b.block.tensor
                def _(tensor, j=j, vN=vN, aB=aB, slot_groups=slot_groups):
                    b.wait(tensor, "v", vN)
                    for gi, (g0, gn) in enumerate(slot_groups):
                        if gi >= 1:
                            b.wait(tensor, "a", aB + gi)
                        else:
                            b.wait(tensor, "a", aB)
                        half = 512 * (gi % 2)
                        ins = None
                        for c in range(gn):
                            ins = nc.tensor.transpose(
                                PB[:, half + 128 * c:half + 128 * (c + 1)],
                                P[:, 0, 128 * (g0 + c):128 * (g0 + c) + 128],
                                b.C["idb"][:])
                        b.inc(ins, "t")

                @b.block.scalar
                def _(scalar, j=j, tB=tB, slot_groups=slot_groups):
                    for gi, (g0, gn) in enumerate(slot_groups):
                        b.wait(scalar, "t", tB + gi + 1)
                        half = 512 * (gi % 2)
                        b.inc(scalar.activation(
                            PT[:, g0:g0 + gn, 128 * j:128 * (j + 1)],
                            PB[:, half:half + 128 * gn].rearrange(
                                "p (c q) -> p c q", c=gn), AF.Copy), "a")

            aPT = b.count("a")
            prev_pb_free = aPT
            pvpar = pvi % 2

            @b.block.tensor
            def _(tensor, aPT=aPT, pvpar=pvpar, maxc=maxc, H=H, vcol=vcol):
                b.wait(tensor, "a", aPT)
                if prev_ctx_done[pvpar] is not None:
                    sem, val = prev_ctx_done[pvpar]
                    b.wait(tensor, sem, val)
                ins = None
                for c in range(maxc):
                    jmin = max(0, c // 2 - 4 * H)
                    ins = nc.tensor.matmul(
                        pp[pvpar][:, 128 * jmin:512],
                        V[:, c, vcol:vcol + 128],
                        PT[:, c, 128 * jmin:512],
                        start=(c == 0), stop=(c == maxc - 1))
                b.inc(ins, "t")

            t_pv = b.count("t")

            if L == 0:
                @b.block.scalar
                def _(scalar, pvpar=pvpar, t_pv=t_pv, sg=prev_cst_dma[pvpar]):
                    b.wait(scalar, "t", t_pv)
                    b.wait(scalar, "s", sg)
                    b.inc(scalar.activation(
                        cstage[:, pvpar, :], pp[pvpar][:, :], AF.Copy), "a")

                ctx_done = ("a", b.count("a"))
                prev_pb_free = prev_pb_free
            else:
                aE2 = b.count("a")
                t_sp = t_pv + 1
                t_pb = t_pv + 2

                @b.block.scalar
                def _(scalar, pvpar=pvpar, t_pv=t_pv, aE2=aE2,
                      t_sp=t_sp, t_pb=t_pb, sg=prev_cst_dma[pvpar]):
                    b.wait(scalar, "t", t_pv)
                    b.inc(scalar.activation(sgb[:, 0, :], pp[pvpar][:, :],
                                            AF.Square), "a")
                    b.wait(scalar, "t", t_sp)
                    scalar.activation(ev[0:1, 0, 0:512], sp[0:1, 0:512], AF.Ln,
                                      bias=b.C["small"][0:1, 0:1], scale=1.0 / 128.0)
                    b.inc(scalar.activation(sgb[0:1, 1, 0:512], ev[0:1, 0, 0:512],
                                            AF.Exp, bias=0.0, scale=-0.5), "a")
                    b.wait(scalar, "t", t_pb)
                    b.inc(scalar.activation(sgb[:, 1, :], sp[:, 0:512], AF.Copy), "a")

                @b.block.tensor
                def _(tensor, aE2=aE2):
                    b.wait(tensor, "a", aE2 + 1)
                    b.inc(nc.tensor.matmul(sp[0:1, 0:512], b.C["onesb_col"][:],
                                           sgb[:, 0, :], start=True, stop=True), "t")
                    b.wait(tensor, "a", aE2 + 2)
                    b.inc(nc.tensor.matmul(sp[:, 0:512], b.C["ones_row_b"][:],
                                           sgb[0:1, 1, 0:512], start=True, stop=True), "t")

                aE3 = b.count("a")

                @b.block.vector
                def _(vector, pvpar=pvpar, aE3=aE3, sg=prev_cst_dma[pvpar]):
                    b.wait(vector, "a", aE3)
                    b.wait(vector, "s", sg)
                    b.inc(vector.scalar_tensor_tensor(
                        cstage[:, pvpar, :], pp[pvpar][:, :], b.C["sweff"][:],
                        sgb[:, 1, :], ALU.mult, ALU.mult), "v")

                ctx_done = ("v", b.count("v"))

            prev_ctx_done[pvpar] = ctx_done

            @b.block.sync
            def _(sync, h=h, H=H, pvpar=pvpar, ctx_done=ctx_done):
                sem, val = ctx_done
                b.wait(sync, sem, val)
                b.inc(sync.dma_start(
                    out=io["ctx_d"][128 * h:128 * (h + 1), 512 * H:512 * (H + 1)],
                    in_=cstage[:, pvpar, :]), "s")

            prev_cst_dma[pvpar] = b.count("s")
            pvi += 1

    b.barrier()



def emit_p8(b, io, L):
    nc = b.nc
    pp = [b.PS["pp0"], b.PS["pp1"]]
    ctx8, gst, xst, ost = b.T["ctx8"], b.T["gst"], b.T["xst"], b.T["ost"]
    ev = b.T["ev"]
    wg = [b.T["wg0"], b.T["wg1"]]

    @b.block.sync
    def _(sync):
        b.inc(sync.dma_start(out=ctx8[:], in_=r128(io["ctx_d"])), "s")

    E = b.snap()
    t_mm = lambda w: E["t"] + w + 1
    v_cmb = lambda w: E["v"] + w + 1

    @b.block.gpsimd
    def _(g):
        for th in range(2):
            for m in range(8):
                if th == 0 and m >= 2:
                    b.wait(g, "t", t_mm(2 * (m - 2) + 1))
                elif th == 1:
                    if m < 2:
                        b.wait(g, "t", t_mm(13 + 2 * m))
                    else:
                        b.wait(g, "t", t_mm(16 + 2 * (m - 2) + 1))
                b.inc(g.dma_start(
                    out=wg[m % 2][:], in_=r128(io["w_o"][:, 256 * m:256 * (m + 1)])), "g")

    for th in range(2):
        @b.block.sync
        def _(sync, th=th):
            if th == 1:
                b.wait(sync, "v", v_cmb(15))
            b.inc(sync.dma_start(
                out=xst[:], in_=r128(io["x_own"][:, 512 * th:512 * (th + 1)])), "s")
            b.inc(sync.dma_start(
                out=r128(io["g_d"][:, 512 * th:512 * (th + 1)]) if False else gst[:],
                in_=r128(io["g_d"][:, 512 * th:512 * (th + 1)])), "s")

        s_loaded = b.count("s")

        for oc in range(16):
            w = 16 * th + oc
            mchunk = oc // 2

            @b.block.tensor
            def _(tensor, th=th, oc=oc, w=w, mchunk=mchunk, s_loaded=s_loaded):
                b.wait(tensor, "g", E["g"] + 16 * (8 * th + mchunk + 1))
                b.wait(tensor, "s", s_loaded)
                if w >= 2:
                    b.wait(tensor, "v", v_cmb(w - 2))
                ocol = 128 * (oc % 2)
                ins = None
                for f in range(FC):
                    ins = nc.tensor.matmul(
                        pp[w % 2][:, :], wg[mchunk % 2][:, f, ocol:ocol + 128],
                        ctx8[:, f, 512 * th:512 * (th + 1)],
                        start=(f == 0), stop=(f == FC - 1))
                b.inc(ins, "t")

            @b.block.vector
            def _(vector, th=th, oc=oc, w=w):
                b.wait(vector, "t", t_mm(w))
                if w >= 4:
                    b.wait(vector, "s", swait_hist[w - 4])
                bo_col = b.C["bo"][:, oc:oc + 1]
                t1 = ev[:, 0, 0:512]
                if L == 0:
                    vector.scalar_tensor_tensor(
                        t1, pp[w % 2][:, :], bo_col, gst[:, oc, :], ALU.add, ALU.mult)
                else:
                    vector.tensor_tensor(t1, pp[w % 2][:, :], gst[:, oc, :], ALU.mult)
                b.inc(vector.tensor_tensor(
                    ost[:, oc % 4, :], t1, xst[:, oc, :], ALU.add), "v")

            v_done = b.count("v")

            @b.block.sync
            def _(sync, th=th, oc=oc, v_done=v_done):
                b.wait(sync, "v", v_done)
                b.inc(sync.dma_start(
                    out=io["x_new"][128 * oc:128 * (oc + 1), 512 * th:512 * (th + 1)],
                    in_=ost[:, oc % 4, :]), "s")

            swait_hist.append(b.count("s"))

    b.barrier()


swait_hist = []


def emit_tail(b):
    s = b.snap()

    @b.block.gpsimd
    def _(g):
        b.waits(g, s)


def emit_layer(b, io, L, upto=99):
    emit_p1(b, io, L)
    if upto >= 2:
        emit_p2(b, io, L)
    if upto >= 3:
        emit_p3(b, io, L)
    if upto >= 4:
        emit_p4(b, io, L)
    if upto >= 5:
        emit_p5(b, io, L)
    if upto >= 6:
        emit_p6(b, io, L)
    if upto >= 7:
        emit_p7(b, io, L)
    if upto >= 8:
        emit_p8(b, io, L)



def to_bf(x):
    return np.ascontiguousarray(np.asarray(x, dtype=np.float32)).astype(ml_dtypes.bfloat16)


def own_token_rows(parity):
    rows = []
    for j in range(8):
        g = 2 * j + parity
        rows.extend(range(128 * g, 128 * (g + 1)))
    return np.array(rows)


def key_order_rows(parity):
    rows = []
    for j in range(8):
        own_g = 2 * j + parity
        oth_g = 2 * j + 1 - parity
        rows.extend(range(128 * own_g, 128 * (own_g + 1)))
        rows.extend(range(128 * oth_g, 128 * (oth_g + 1)))
    return np.array(rows)


_CONST_CACHE = {}


def make_consts(parity):
    if parity in _CONST_CACHE:
        return _CONST_CACHE[parity]
    c = {}
    c["c_idb"] = to_bf(np.eye(128, dtype=np.float32))
    c["c_idf"] = np.eye(128, dtype=np.float32)
    basis = np.zeros((8, 1024), np.float32)
    for e in range(8):
        basis[e, 128 * e:128 * (e + 1)] = 1.0
    c["c_basis"] = to_bf(basis)
    prot = np.zeros((128, 128), np.float32)
    for blk in range(2):
        for m in range(64):
            if m < 32:
                prot[blk * 64 + m + 32, blk * 64 + m] = -1.0
            else:
                prot[blk * 64 + m - 32, blk * 64 + m] = 1.0
    c["c_prot"] = to_bf(prot)
    mask = np.zeros((128, 8, 256), np.float32)
    r = np.arange(128)
    tri = np.where(np.arange(128)[None, :] <= r[:, None], 0.0, -30000.0)
    mask[:, :, 0:128] = tri[:, None, :]
    mask[:, :, 128:256] = -30000.0 if parity == 0 else 0.0
    c["c_mask"] = to_bf(mask).reshape(128, 8 * 256)
    dh = 64
    inv_freq = 10000.0 ** (-np.arange(dh, dtype=np.float64) / dh)
    pos = np.arange(TF, dtype=np.float64)
    fr = pos[:, None] * inv_freq[None, :]
    cosT = np.cos(fr).T.astype(np.float32)
    sinT = np.sin(fr).T.astype(np.float32)
    own = own_token_rows(parity)
    key = key_order_rows(parity)
    c["c_cosq"] = to_bf(np.tile(cosT[:, own], (2, 1)))
    c["c_sinq"] = to_bf(np.tile(sinT[:, own], (2, 1)))
    c["c_cosk"] = to_bf(np.tile(cosT[:, key], (2, 1)))
    c["c_sink"] = to_bf(np.tile(sinT[:, key], (2, 1)))
    _CONST_CACHE[parity] = c
    return c


def prep_layer_weights(L, inputs):
    w = {}
    A = lambda x: np.ascontiguousarray(np.asarray(x, dtype=np.float32))
    if L == 0:
        w["w_q"] = A(inputs["L0_attn_Wq"])
        w["w_k"] = A(inputs["L0_attn_Wk"])
        w["w_v"] = A(inputs["L0_attn_Wv"])
        w["w_o"] = A(inputs["L0_attn_Wo"])
        w["c_bo"] = A(np.asarray(inputs["L0_attn_bo"]).reshape(16, 128).T)
        w["c_lamv"] = np.zeros((1, 256), np.float32)
        w["c_subw"] = np.zeros((128, 1), np.float32)
        pre = "L0_ffn_"
    else:
        wqT = np.asarray(inputs["L1_attn_Wq"], np.float32).T
        perm = np.empty(2048, np.int64)
        for c in range(16):
            for bb in range(2):
                for d in range(64):
                    j = 4 * (c // 2) + 2 * bb + (c % 2)
                    perm[128 * c + 64 * bb + d] = 64 * j + d
        w["w_q"] = A(wqT[:, perm])
        w["w_k"] = A(np.asarray(inputs["L1_attn_Wk"]).T)
        w["w_v"] = A(np.asarray(inputs["L1_attn_Wv"]).T)
        w["w_o"] = A(np.asarray(inputs["L1_attn_Wo"]).T)
        w["c_bo"] = np.zeros((128, 16), np.float32)
        lam = np.concatenate([np.asarray(inputs["L1_attn_lambda_q1"]),
                              np.asarray(inputs["L1_attn_lambda_k1"]),
                              np.asarray(inputs["L1_attn_lambda_q2"]),
                              np.asarray(inputs["L1_attn_lambda_k2"])])
        w["c_lamv"] = A(lam.reshape(1, 256))
        w["c_subw"] = A(np.asarray(inputs["L1_attn_subln_weight"]).reshape(128, 1))
        pre = "L1_ffn_"
    w["w_rwT"] = A(np.asarray(inputs[pre + "router_w"]).T)
    gT, uT, dT = [], [], []
    for e in range(8):
        gT.append(np.asarray(inputs[pre + "w_gate"][e]).T)
        uT.append(np.asarray(inputs[pre + "w_up"][e]).T)
        dT.append(np.asarray(inputs[pre + "w_down"][e]).T)
    gT.append(np.asarray(inputs[pre + "shared_gate"]).T)
    uT.append(np.asarray(inputs[pre + "shared_up"]).T)
    dT.append(np.asarray(inputs[pre + "shared_down"]).T)
    w["w_gT"] = A(np.stack(gT))
    w["w_uT"] = A(np.stack(uT))
    w["w_dT"] = A(np.stack(dT))
    return w


def declare_layer_io(nc, L, dbg=()):
    io = {}
    dt = nc.dram_tensor
    io["x_own"] = dt("x_own", [D, T], F32, kind="ExternalInput").ap()
    io["x_oth"] = dt("x_oth", [D, T], F32, kind="ExternalInput").ap()
    io["w_q"] = dt("w_q", [D, D], F32, kind="ExternalInput").ap()
    ko = 512 if L == 0 else 1024
    io["w_k"] = dt("w_k", [D, ko], F32, kind="ExternalInput").ap()
    io["w_v"] = dt("w_v", [D, ko], F32, kind="ExternalInput").ap()
    io["w_o"] = dt("w_o", [D, D], F32, kind="ExternalInput").ap()
    io["w_rwT"] = dt("w_rwT", [D, 8], F32, kind="ExternalInput").ap()
    wgt = dt("w_gT", [9, D, DI], F32, kind="ExternalInput").ap()
    wut = dt("w_uT", [9, D, DI], F32, kind="ExternalInput").ap()
    wdt = dt("w_dT", [9, DI, D], F32, kind="ExternalInput").ap()
    io["w_gT"] = [wgt[e] for e in range(9)]
    io["w_uT"] = [wut[e] for e in range(9)]
    io["w_dT"] = [wdt[e] for e in range(9)]
    for name, shape, dtyp in (
        ("c_idb", [128, 128], BF16), ("c_idf", [128, 128], F32),
        ("c_basis", [8, 1024], BF16), ("c_prot", [128, 128], BF16),
        ("c_mask", [128, 8 * 256], BF16), ("c_bo", [128, 16], F32),
        ("c_subw", [128, 1], F32), ("c_lamv", [1, 256], F32),
        ("c_cosq", [128, 1024], BF16), ("c_sinq", [128, 1024], BF16),
        ("c_cosk", [128, 2048], BF16), ("c_sink", [128, 2048], BF16),
    ):
        io[name] = dt(name, shape, dtyp, kind="ExternalInput").ap()
    io["c_mask"] = io["c_mask"].rearrange("p (j k) -> p j k", j=8)
    io["xn_own_d"] = dt("xn_own_d", [D, T], BF16, kind="Internal").ap()
    io["xn_oth_d"] = dt("xn_oth_d", [D, T], BF16, kind="Internal").ap()
    io["g_d"] = dt("g_d", [D, T], BF16, kind="Internal").ap()
    io["ctx_d"] = dt("ctx_d", [D, T], BF16,
                     kind="ExternalOutput" if "d_ctx" in dbg else "Internal").ap()
    io["x_new"] = dt("x_new", [D, T], F32, kind="ExternalOutput").ap()
    for d in dbg:
        if d == "d_ctx":
            continue
        shape = [D, T]
        dtyp = F32
        if d == "d_comb":
            shape = [8, 1024]
            dtyp = BF16
        if d == "d_lg":
            shape = [8, 1024]
            dtyp = F32
        if d == "d_wexp":
            shape = [128, 64]
            dtyp = F32
        if d == "d_lgt":
            shape = [128, 64]
            dtyp = F32
        if d == "d_lgs":
            shape = [128, 64]
            dtyp = F32
        if d == "d_p7":
            shape = [128, 2048]
            dtyp = BF16
        if d == "d_sums":
            shape = [128, 32]
            dtyp = F32
        if d == "d_lam":
            shape = [128, 8]
            dtyp = F32
        if d in ("d_xn", "d_q", "d_v"):
            dtyp = BF16
        if d == "d_k":
            shape = [512 if L == 0 else 1024, 2048]
            dtyp = BF16
        io[d] = dt(d, shape, dtyp, kind="ExternalOutput").ap()
    srcs = []
    for posj in range(16):
        j = posj // 2
        base = io["xn_own_d"] if posj % 2 == 0 else io["xn_oth_d"]
        srcs.append(base[:, 128 * j:128 * (j + 1)].rearrange("(c p) t -> p c t", p=128))
    io["kv_src"] = srcs
    return io


def build_layer_nc(L, upto=99, dbg=()):
    global swait_hist
    swait_hist = []
    import os
    nc = bass.Bass("TRN2", target_bir_lowering=False,
                   detect_race_conditions=os.environ.get("KBRACE", "0") == "1")
    es = contextlib.ExitStack()
    with es:
        b = Build(nc, es)
        io = declare_layer_io(nc, L, dbg)
        b.setup()
        emit_consts(b, io)
        emit_layer(b, io, L, upto)
        emit_tail(b)
    return nc



def kernel(**inputs):
    x0 = np.asarray(inputs["L0_attn_x"], dtype=np.float32)
    rows = [own_token_rows(p) for p in (0, 1)]

    cur = []
    for core in range(8):
        seq, par = core // 2, core % 2
        xt = np.ascontiguousarray(x0[seq].T)
        cur.append(np.ascontiguousarray(xt[:, rows[par]]))

    for L in (0, 1):
        w = prep_layer_weights(L, inputs)
        nc = build_layer_nc(L)
        in_maps = []
        for core in range(8):
            par = core % 2
            m = {}
            for k, v in w.items():
                m[k] = v
            for k, v in make_consts(par).items():
                m[k] = v
            m["c_bo"] = w["c_bo"]
            m["c_lamv"] = w["c_lamv"]
            m["c_subw"] = w["c_subw"]
            m["x_own"] = cur[core]
            m["x_oth"] = cur[core ^ 1]
            in_maps.append(m)
        res = run_bass_kernel_spmd(nc, in_maps, core_ids=list(range(8)))
        cur = [np.ascontiguousarray(res.results[c]["x_new"]) for c in range(8)]

    out = np.empty((4, 2048, 2048), np.float32)
    for core in range(8):
        seq, par = core // 2, core % 2
        out[seq][rows[par], :] = cur[core].T
    return out
